# version 03
# baseline (speedup 1.0000x reference)
import contextlib
import os
import numpy as np
import concourse.bass as bass
import concourse.mybir as mybir
from concourse.bass_utils import run_bass_kernel_spmd

F32 = mybir.dt.float32
BF16 = mybir.dt.bfloat16
I32 = mybir.dt.int32
AF = mybir.ActivationFunctionType
ALU = mybir.AluOpType
AXX = mybir.AxisListType.X

P = 128
D = 2048
T = 2048
KT = 16
NB = 2
H = 16
INTER = 5504
IT = 43
EPS = 1e-5
FCH = 1024
MAGIC = 0x5F3759DF
SW = 64

ENGINES = ("sync", "scalar", "vector", "tensor", "gpsimd")


class Sem:
    __slots__ = ("h", "n")

    def __init__(self, handle):
        self.h = handle
        self.n = 0


class Prog:
    def __init__(self, nc, ctx):
        self.nc = nc
        self.ctx = ctx
        self.ops = {e: [] for e in ENGINES}
        self._sems = {}
        self.bar = {e: self.sem(f"bar_{e}") for e in ENGINES}

    def sem(self, name):
        if name not in self._sems:
            self._sems[name] = Sem(self.ctx.enter_context(self.nc.semaphore(name)))
        return self._sems[name]

    def emit(self, eng, fn):
        self.ops[eng].append(fn)

    def wait(self, eng, sem, tgt):
        if tgt > 0:
            self.emit(eng, lambda e, s=sem.h, t=tgt: e.wait_ge(s, t))

    def op_preinc(self, eng, fn, sem, k=1):
        self.emit(eng, lambda e, f=fn, s=sem.h, kk=k: f(e).then_inc(s, kk))

    def op(self, eng, fn, inc=None, k=1):
        if inc is not None:
            inc.n += k
            self.emit(eng, lambda e, f=fn, s=inc.h, kk=k: f(e).then_inc(s, kk))
            return inc.n
        self.emit(eng, fn)
        return None

    def barrier(self):
        self.barrier_hook(self)

    def finalize(self, block):
        for e in ENGINES:
            body = self.ops[e]
            getattr(block, e)(lambda engine, bd=body: [f(engine) for f in bd])




class Arena:
    def __init__(self, nc, ctx, nbytes):
        self.t = ctx.enter_context(nc.sbuf_tensor("arena", [P, nbytes // 4], F32))
        self.top = 0
        self.nbytes = nbytes
        self._marks = []
        self.peak = 0

    def alloc(self, nelem, dtype):
        esz = {F32: 4, I32: 4, BF16: 2}[dtype]
        off = (self.top + 3) // 4 * 4
        nb = nelem * esz
        assert off + nb <= self.nbytes, f"arena overflow {off}+{nb}"
        self.top = off + nb
        self.peak = max(self.peak, self.top)
        base = self.t[:, off // 4:(off + nb + 3) // 4]
        if dtype is F32:
            return base[:, :nelem]
        return base.bitcast(dtype)[:, :nelem]

    def mark(self):
        self._marks.append(self.top)

    def release(self):
        self.top = self._marks.pop()


class Ring:

    def __init__(self, prog, banks, relsem):
        self.P = prog
        self.banks = banks
        self.rel = relsem
        self.free_at = [0] * len(banks)
        self.ptr = 0
        self.q = []

    def acquire(self):
        i = self.ptr % len(self.banks)
        self.ptr += 1
        self.P.wait("tensor", self.rel, self.free_at[i])
        self.q.append(i)
        return self.banks[i]

    def plan_release(self):
        i = self.q.pop(0)
        self.rel.n += 1
        self.free_at[i] = self.rel.n
        return self.rel



def _lhsT_tiles(wT):
    Kd, M = wT.shape
    return np.ascontiguousarray(wT.reshape(Kd // P, P, M // P, P).transpose(2, 0, 1, 3))


def _rhs_tiles(wT):
    Kd, N = wT.shape
    return np.ascontiguousarray(wT.reshape(Kd // P, P, N))


def _row_bc(v):
    return np.ascontiguousarray(np.tile(np.asarray(v)[None, :], (P, 1)))


def _pp(v):
    return np.ascontiguousarray(np.asarray(v).reshape(-1, P).T)



def build_nc():
    nc = bass.Bass("TRN2", target_bir_lowering=False, detect_race_conditions=False)

    def din(name, shape, dt=F32):
        return nc.dram_tensor(name, list(shape), dt, kind="ExternalInput").ap()

    def dscr(name, shape, dt):
        return nc.dram_tensor(name, list(shape), dt, kind="Internal").ap()

    x_in = din("x_in", [2 * KT, P, D])
    lhsA = {m: din(f"lhs_{m}", [16, KT, P, P]) for m in ("A", "Dm", "B", "Ca", "Cb")}
    rhs_outw = din("rhs_outw", [KT, P, D])
    rhs_q = din("rhs_q", [KT, P, D])
    rhs_k = din("rhs_k", [KT, P, D])
    rhs_v = din("rhs_v", [KT, P, D])
    rhs_og = din("rhs_og", [KT, P, D])
    rhs_wo = din("rhs_wo", [KT, P, D])
    lhs_ig = din("lhs_ig", [16, KT, P, P])
    lhs_gw = din("lhs_gw", [KT, P, H])
    gidxT = din("gidxT", [KT, P, INTER], I32)
    uidxT = din("uidxT", [KT, P, INTER], I32)
    didxT = din("didxT", [IT, P, D], I32)
    lut_t = din("lut_t", [P, 16])
    ab_pp = din("ab_pp", [P, 16])
    outb_bc = din("outb_bc", [P, D])
    rms_bc = din("rms_bc", [P, D])
    igb_pp = din("igb_pp", [P, 16])
    convw_pp = din("convw_pp", [P, 64])
    convb_pp = din("convb_pp", [P, 16])
    gammab_pp = din("gammab_pp", [16, 1])
    ogb_bc = din("ogb_bc", [P, D])
    vng_bc = din("vng_bc", [P, D])
    vnb_bc = din("vnb_bc", [P, D])
    mng_bc = din("mng_bc", [P, D])
    mnb_bc = din("mnb_bc", [P, D])
    gng_bc = din("gng_bc", [P, D])
    gnb_bc = din("gnb_bc", [P, D])
    y_out = nc.dram_tensor("y_out", [2 * KT, P, D], F32, kind="ExternalOutput").ap()
    dbg_on = os.environ.get("MK_DBG", "") != ""
    if dbg_on:
        dbg1 = nc.dram_tensor("dbg1", [KT, P, T], BF16, kind="ExternalOutput").ap()
        dbg2 = nc.dram_tensor("dbg2", [16, P, T], BF16, kind="ExternalOutput").ap()
        dbg3 = nc.dram_tensor("dbg3", [16, P, D], F32, kind="ExternalOutput").ap()

    x_scr = dscr("x_scr", [2 * KT, P, D], F32)
    wg_scr = dscr("wg_scr", [IT, KT, P, P], BF16)
    wu_scr = dscr("wu_scr", [IT, KT, P, P], BF16)
    wd_scr = dscr("wd_scr", [4, IT, P, 512], BF16)
    y_scr = dscr("y_scr", [16, P, T], BF16)
    q_scr = dscr("q_scr", [16, P, D], BF16)
    og_scr = dscr("og_scr", [16, P, D], BF16)
    k_scr = dscr("k_scr", [16, 16, P, P], BF16)
    v_scr = dscr("v_scr", [16, 16, P, P], BF16)
    ig_scr = dscr("ig_scr", [16, 16, P, P], BF16)
    gam_scr = dscr("gam_scr", [16, T], F32)
    mem_scr = dscr("mem_scr", [16, P, D], F32)

    ctx = contextlib.ExitStack()
    with ctx:
        PR = Prog(nc, ctx)
        A = Arena(nc, ctx, 204 * 1024)
        psf = [ctx.enter_context(nc.psum_tensor(f"psf{i}", [P, 512], F32)) for i in range(6)]
        psb = [ctx.enter_context(nc.psum_tensor(f"psb{i}", [P, 1024], BF16)) for i in range(2)]
        ringF = Ring(PR, [p[:] for p in psf], PR.sem("relf"))
        ringB = Ring(PR, [p[:] for p in psb], PR.sem("relb"))

        dsy = PR.sem("dsy")
        dgp = PR.sem("dgp")
        msc = PR.sem("msc")
        mvc = PR.sem("mvc")
        mtc = PR.sem("mtc")
        mgp = PR.sem("mgp")

        def _barrier(PRX):
            PRX.op("scalar", lambda e: e.copy(out=dumm[:, 0:1], in_=dumm[:, 0:1]), inc=msc)
            PRX.op("vector", lambda e: e.engine_nop(), inc=mvc)
            PRX.op("gpsimd", lambda e: e.memset(dumm[:, 1:2], 0.0), inc=mgp)
            tg = [(dsy, dsy.n), (dgp, dgp.n), (msc, msc.n), (mvc, mvc.n),
                  (mtc, mtc.n), (mgp, mgp.n)]
            for e in ENGINES:
                for s, t in tg:
                    PRX.wait(e, s, t)
        PR.barrier_hook = _barrier

        ident_b = A.alloc(P, BF16)
        ident_f = A.alloc(P, F32)
        lut_sb = A.alloc(16, F32)
        dk2 = A.alloc(16, F32)
        ab_sb = A.alloc(16, F32)
        igb_sb = A.alloc(16, F32)
        convw_sb = A.alloc(64, F32)
        convb_sb = A.alloc(16, F32)
        gammab_sb = A.alloc(1, F32)
        kbias = A.alloc(16, F32)
        stats = A.alloc(32 + SW, F32)
        carryB = A.alloc(16, F32)
        gscr = A.alloc(32, F32)
        dumm = A.alloc(2, F32)
        s_m = A.alloc(SW, F32)
        s_v = A.alloc(SW, F32)
        s_r = A.alloc(SW, F32)
        s_g = A.alloc(SW, F32)
        s_f = A.alloc(SW, F32)
        s_nm = A.alloc(SW, F32)
        s_t1 = A.alloc(SW, F32)
        s_t2 = A.alloc(SW, F32)
        nwt1 = A.alloc(32, F32)
        nwt2 = A.alloc(32, F32)
        bn6 = A.alloc(24, F32)
        mv2 = A.alloc(2, F32)

        for dst, src in [(lut_sb, lut_t), (ab_sb, ab_pp), (igb_sb, igb_pp),
                         (convw_sb, convw_pp), (convb_sb, convb_pp)]:
            PR.op("sync", lambda e, d=dst, s=src: e.dma_start(out=d, in_=s), inc=dsy, k=16)
        PR.op("sync", lambda e: e.dma_start(out=gammab_sb[0:16, :], in_=gammab_pp),
              inc=dsy, k=16)
        consts_t = dsy.n

        def _ginit(g):
            g.memset(ident_b, 0.0).then_inc(mgp.h, 1)
            g.wait_ge(mgp.h, 1)
            g.affine_select(out=ident_b, in_=ident_b, compare_op=ALU.not_equal,
                            fill=1.0, base=0, pattern=[[-1, P]], channel_multiplier=1)
            g.memset(ident_f, 0.0).then_inc(mgp.h, 1)
            g.wait_ge(mgp.h, 2)
            g.affine_select(out=ident_f, in_=ident_f, compare_op=ALU.not_equal,
                            fill=1.0, base=0, pattern=[[-1, P]], channel_multiplier=1)
            for k in range(1, 16):
                g.memset(kbias[:, k:k + 1], -(k - 0.5))
            g.memset(dumm, 0.0)
            g.memset(gscr, 0.0)
            g.memset(nwt1, 0.0)
            g.memset(nwt2, 0.0)
            g.memset(carryB, 0.0)
            for bufz in (s_m, s_v, s_r, s_g, s_f, s_nm, s_t1, s_t2, stats):
                g.memset(bufz, 1.0)
            return g.memset(kbias[:, 0:1], 0.0)
        assert mgp.n == 0
        mgp.n += 2
        ginit_t = PR.op("gpsimd", _ginit, inc=mgp)
        PR.wait("tensor", mgp, ginit_t)
        PR.wait("scalar", mgp, ginit_t)

        PR.wait("vector", dsy, consts_t)

        _f0 = mvc.n

        def _vdk(v, f=_f0):
            v.tensor_sub(out=dk2[:, 1:16], in0=lut_sb[:, 1:16], in1=lut_sb[:, 0:15]) \
                .then_inc(mvc.h, 1)
            v.wait_ge(mvc.h, f + 1)
            return v.tensor_copy(out=dk2[:, 0:1], in_=lut_sb[:, 0:1])
        mvc.n += 1
        PR.op("vector", _vdk, inc=mvc)
        dk_done = mvc.n

        def load_row(dramsrc):
            buf = A.alloc(D, F32)
            t = PR.op("sync", lambda e, d=buf, s=dramsrc: e.dma_start(out=d, in_=s),
                      inc=dsy, k=16)
            return buf, t

        def rsqrt_emit(v, out, in_, t1=None, w=SW):
            assert w >= SW
            if t1 is None:
                t1 = s_t2
            oi, ii = out[:, 0:w].bitcast(I32), in_[:, 0:w].bitcast(I32)
            v.tensor_single_scalar(out=oi, in_=ii, scalar=1, op=ALU.logical_shift_right)
            v.tensor_single_scalar(out=oi, in_=oi, scalar=MAGIC, op=ALU.subtract)
            v.tensor_single_scalar(out=oi, in_=oi, scalar=-1, op=ALU.bitwise_xor)
            v.tensor_single_scalar(out=oi, in_=oi, scalar=1, op=ALU.add)
            o, i_, a1 = out[:, 0:w], in_[:, 0:w], t1[:, 0:w]
            for _ in range(3):
                v.tensor_mul(out=a1, in0=o, in1=o)
                v.tensor_mul(out=a1, in0=a1, in1=i_)
                v.tensor_scalar(out=a1, in0=a1, scalar1=-0.5, scalar2=1.5,
                                op0=ALU.mult, op1=ALU.add)
                v.tensor_mul(out=o, in0=o, in1=a1)

        Wst = {}

        def make_W_pump():
            idx_b = [A.alloc(D, I32) for _ in range(2)]
            idxf = A.alloc(D, F32)
            msk = [A.alloc(D, BF16) for _ in range(2)]
            wrow = [A.alloc(D, BF16), None]
            wrow[1] = wrow[0]
            mp = PR.sem("w_mp")
            mc = PR.sem("w_mc")
            rows = []
            for cg, (c0, w) in enumerate(((0, 2048), (2048, 2048), (4096, 1408))):
                for kt in range(KT):
                    rows.append(("g", kt, c0, w))
                    rows.append(("u", kt, c0, w))
            for r in range(IT):
                rows.append(("d", r, 0, D))
            st = {"ri": 0, "nm": 0, "cast_t": [], "wout_t": []}
            PR.wait("vector", mvc, dk_done)

            def pump(kmax):
                n = 0
                while n < kmax and st["ri"] < len(rows):
                    ri = st["ri"]
                    kind, r, c0, w = rows[ri]
                    buf = ri % 2
                    src2 = {"g": gidxT, "u": uidxT, "d": didxT}[kind]
                    if ri >= 2:
                        PR.wait("sync", mgp, st["cast_t"][ri - 2])
                    ldt = PR.op("sync", lambda e, d=idx_b[buf][:, 0:w],
                                s=src2[r][:, c0:c0 + w]:
                                e.dma_start(out=d, in_=s), inc=dsy, k=16)
                    PR.wait("gpsimd", dsy, ldt)
                    st["cast_t"].append(PR.op("gpsimd", lambda e, d=idxf[:, 0:w],
                                              s=idx_b[buf][:, 0:w]:
                                              e.tensor_copy(out=d, in_=s), inc=mgp))
                    for k in range(1, 16):
                        st["nm"] += 1
                        nm = st["nm"]
                        mb = msk[nm % 2]
                        if nm > 2:
                            PR.wait("gpsimd", mc, nm - 2)
                        PR.op("gpsimd", lambda e, d=mb[:, 0:w], s=idxf[:, 0:w], kk=k:
                              e.tensor_single_scalar(out=d, in_=s, scalar=float(kk),
                                                     op=ALU.is_ge), inc=mp)
                    if ri >= 1:
                        PR.wait("vector", dgp, st["wout_t"][ri - 1])
                    nm0 = st["nm"] - 15
                    for k in range(1, 16):
                        nm = nm0 + k
                        mb = msk[nm % 2]
                        PR.wait("vector", mp, nm)
                        if k == 1:
                            PR.op("vector", lambda e, o=wrow[buf][:, 0:w], s=mb[:, 0:w], kk=k:
                                  e.tensor_scalar(out=o, in0=s, scalar1=dk2[:, kk:kk + 1],
                                                  scalar2=dk2[:, 0:1], op0=ALU.mult,
                                                  op1=ALU.add), inc=mc)
                        else:
                            PR.op("vector", lambda e, o=wrow[buf][:, 0:w], s=mb[:, 0:w], kk=k:
                                  e.scalar_tensor_tensor(out=o, in0=s,
                                                         scalar=dk2[:, kk:kk + 1], in1=o,
                                                         op0=ALU.mult, op1=ALU.add), inc=mc)
                    vt = mc.n
                    PR.wait("gpsimd", mc, vt)
                    if kind in ("g", "u"):
                        scr = wg_scr if kind == "g" else wu_scr
                        for it in range(c0 // P, (c0 + w) // P):
                            PR.op("gpsimd", lambda e, d=scr[it, r],
                                  s=wrow[buf][:, it * P - c0:(it + 1) * P - c0]:
                                  e.dma_start(out=d, in_=s), inc=dgp, k=16)
                    else:
                        for dg in range(4):
                            PR.op("gpsimd", lambda e, d=wd_scr[dg, r],
                                  s=wrow[buf][:, dg * 512:(dg + 1) * 512]:
                                  e.dma_start(out=d, in_=s), inc=dgp, k=16)
                    st["wout_t"].append(dgp.n)
                    st["ri"] += 1
                    n += 1

            def drain():
                pump(10 ** 9)
                return dgp.n
            return pump, drain

        def ln_transpose(b, x_src, xnT, tiles, rms_mode, save_stats, tag, rms_buf=None, pump_hook=None):
            lvl = int(os.environ.get("MK_LN_LVL", "3"))
            xt = [A.alloc(D, F32) for _ in range(2)]
            sqb = [A.alloc(D, F32) for _ in range(2)]
            xnb = [A.alloc(D, BF16) for _ in range(2)]
            xn32 = A.alloc(D, F32)
            ncols = len(tiles) * P
            vdone, tpdone = [], []
            for j, tt in enumerate(tiles):
                g = b * 16 + tt
                buf = j % 2
                if j >= 2:
                    PR.wait("sync", mvc, vdone[j - 2])
                ldt = PR.op("sync", lambda e, d=xt[buf], s=x_src[g]:
                            e.dma_start(out=d, in_=s), inc=dsy, k=16)
                if lvl == 0:
                    vdone.append(PR.op("vector", lambda e: e.engine_nop(), inc=mvc))
                    tpdone.append(mtc.n)
                    continue
                PR.wait("scalar", dsy, ldt)
                if j >= 2:
                    PR.wait("scalar", mvc, vdone[j - 2])
                sq_t = PR.op("scalar", lambda e, d=sqb[buf], s=xt[buf]:
                             e.activation(out=d, in_=s, func=AF.Square), inc=msc)
                PR.wait("vector", dsy, ldt)
                if j >= 2:
                    PR.wait("vector", mtc, tpdone[j - 2])

                ft = mvc.n + 1

                def _v(v, X=xt[buf], SQ=sqb[buf], XN=xnb[buf], tt=tt, sq_t=sq_t, ft=ft):
                    v.reduce_sum(out=s_m[:, 0:1], in_=X, axis=AXX)
                    v.wait_ge(msc.h, sq_t)
                    v.reduce_sum(out=s_v[:, 0:1], in_=SQ, axis=AXX).then_inc(mvc.h, 1)
                    v.wait_ge(mvc.h, ft)
                    v.tensor_scalar_mul(out=s_m, in0=s_m, scalar1=1.0 / D)
                    v.tensor_scalar_mul(out=s_v, in0=s_v, scalar1=1.0 / D)
                    v.tensor_mul(out=s_t1, in0=s_m, in1=s_m)
                    v.tensor_sub(out=s_v, in0=s_v, in1=s_t1)
                    v.tensor_scalar_add(out=s_t1, in0=s_v, scalar1=EPS)
                    rsqrt_emit(v, s_r, s_t1)
                    if rms_mode:
                        v.tensor_mul(out=s_g, in0=s_v, in1=s_r)
                        v.tensor_mul(out=s_g, in0=s_g, in1=s_r)
                        v.tensor_scalar_add(out=s_g, in0=s_g, scalar1=EPS)
                        rsqrt_emit(v, s_t1, s_g)
                        v.tensor_mul(out=s_f, in0=s_r, in1=s_t1)
                        fb = s_f
                    else:
                        fb = s_r
                    if save_stats:
                        v.tensor_copy(out=stats[:, tt:tt + 1], in_=s_m[:, 0:1])
                        v.tensor_copy(out=stats[:, 16 + tt:17 + tt], in_=s_r[:, 0:1])
                    v.tensor_mul(out=s_nm, in0=s_m, in1=fb)
                    v.tensor_scalar_mul(out=s_nm, in0=s_nm, scalar1=-1.0)
                    if rms_mode:
                        v.tensor_scalar(out=xn32, in0=X, scalar1=fb[:, 0:1],
                                        scalar2=s_nm[:, 0:1], op0=ALU.mult, op1=ALU.add)
                        return v.tensor_mul(out=XN, in0=xn32, in1=rms_buf)
                    return v.tensor_scalar(out=XN, in0=X, scalar1=fb[:, 0:1],
                                           scalar2=s_nm[:, 0:1], op0=ALU.mult, op1=ALU.add)
                mvc.n += 1
                vdone.append(PR.op("vector", _v, inc=mvc))
                if lvl == 1:
                    tpdone.append(mtc.n)
                    continue
                PR.wait("tensor", mvc, vdone[-1])
                for half in range(2):
                    bank = ringB.acquire()
                    for jj in range(8):
                        kt = half * 8 + jj
                        PR.op("tensor", lambda e, o=bank[:, jj * P:(jj + 1) * P],
                              i=xnb[buf][:, kt * P:(kt + 1) * P]:
                              e.transpose(o, i, ident_b),
                              inc=mtc if jj == 7 else None)
                    tpt = mtc.n
                    PR.wait("scalar", mtc, tpt)
                    dst = xnT.rearrange("p (kt t) -> p kt t", t=ncols)[:, half * 8:half * 8 + 8,
                                                                      j * P:(j + 1) * P]
                    srcv = bank.rearrange("p (k t) -> p k t", k=8)
                    rs = ringB.plan_release()
                    if lvl >= 3:
                        PR.op_preinc("scalar", lambda e, d=dst, s=srcv: e.copy(out=d, in_=s), rs)
                    else:
                        PR.op_preinc("scalar", lambda e: e.copy(
                            out=dumm[:, 0:1], in_=dumm[:, 0:1]), rs)
                tpdone.append(mtc.n)
                if pump_hook:
                    pump_hook(2)

        def phase_A(b, with_W=False):
            a_sub = os.environ.get("MK_A_SUB", "123")
            pump = None
            if with_W:
                A.mark()
                pump, drain = make_W_pump()
            A.mark()
            xnT = A.alloc(KT * T, BF16)
            A.mark()
            rms_sb, rms_t = load_row(rms_bc)
            PR.wait("vector", dsy, rms_t)
            ln_transpose(b, x_in, xnT, list(range(16)), rms_mode=True,
                         save_stats=True, tag=f"a1_{b}", rms_buf=rms_sb, pump_hook=pump)
            PR.barrier()
            if dbg_on and b == 0:
                for kt in range(KT):
                    PR.op("sync", lambda e, d=dbg1[kt], s=xnT[:, kt * T:(kt + 1) * T]:
                          e.dma_start(out=d, in_=s), inc=dsy, k=16)
                PR.op("sync", lambda e: e.dma_start(out=dbg2[0][:, 0:128],
                      in_=stats[:, 0:64].bitcast(BF16)), inc=dsy, k=16)
                PR.wait("sync", dsy, dsy.n)
            A.release()
            if "2" not in a_sub:
                A.release()
                return

            mats = ("A", "Dm", "B", "Ca", "Cb")
            NWB = 2
            wbuf = {m: [A.alloc(KT * P, BF16) for _ in range(NWB)] for m in mats}
            a_f = A.alloc(T, F32)
            u_f = A.alloc(T, F32)
            h_f = A.alloc(T, F32)
            ds_f = A.alloc(T, F32)
            sg_f = A.alloc(T, F32)
            sil_f = A.alloc(T, F32)
            c_b = A.alloc(T, BF16)
            y_b = A.alloc(T, BF16)
            ngd = PR.sem(f"a2ng{b}")
            ywr = []
            for ng in range(16):
                buf = ng % NWB
                if ng >= NWB:
                    PR.wait("gpsimd", ngd, ng - NWB + 1)
                for m in mats:
                    PR.op("gpsimd", lambda e,
                          d=wbuf[m][buf].rearrange("p (kt m) -> p kt m", kt=KT),
                          s=lhsA[m][ng].rearrange("kt p m -> p kt m"):
                          e.dma_start(out=d, in_=s), inc=dgp, k=16)
                wt = dgp.n
                PR.wait("tensor", dgp, wt)
                for m in mats:
                    for sr in range(2):
                        bks = [ringF.acquire(), ringF.acquire()]
                        for kt in range(KT):
                            for half in range(2):
                                ts = sr * 2 + half
                                PR.op("tensor", lambda e, o=bks[half][:, 0:512],
                                      l=wbuf[m][buf][:, kt * P:(kt + 1) * P],
                                      r=xnT[:, kt * T + ts * 512:kt * T + (ts + 1) * 512],
                                      st=(kt == 0), sp=(kt == KT - 1):
                                      e.matmul(o, l, r, start=st, stop=sp),
                                      inc=mtc if (kt == KT - 1 and half == 1) else None)
                        mm_t = mtc.n
                        for half in range(2):
                            ts = sr * 2 + half
                            sl = slice(ts * 512, (ts + 1) * 512)
                            bank = bks[half]
                            rs = None
                            if m == "A":
                                PR.wait("scalar", mtc, mm_t)
                                if ng >= 1:
                                    PR.wait("scalar", ngd, ng)
                                rs = ringF.plan_release()
                                PR.op_preinc("scalar", lambda e, o=a_f[:, sl], s=bank[:, 0:512], nn=ng:
                                      e.activation(out=o, in_=s, func=AF.Sigmoid,
                                                   bias=ab_sb[:, nn:nn + 1]), rs)
                            elif m == "Dm":
                                PR.wait("scalar", mtc, mm_t)
                                rs = ringF.plan_release()
                                PR.op_preinc("scalar", lambda e, o=ds_f[:, sl], s=bank[:, 0:512]:
                                      e.activation(out=o, in_=s, func=AF.Sigmoid), rs)
                                if sr == 1 and half == 1:
                                    ad_done = PR.op("scalar", lambda e: e.copy(
                                        out=dumm[:, 0:1], in_=dumm[:, 0:1]), inc=msc)
                            elif m == "B":
                                PR.wait("vector", mtc, mm_t)
                                if sr == 0 and half == 0:
                                    PR.wait("vector", msc, ad_done)
                                rs = ringF.plan_release()
                                PR.op_preinc("vector", lambda e, o=u_f[:, sl], s=bank[:, 0:512],
                                      d2=ds_f[:, sl]: e.tensor_mul(out=o, in0=d2, in1=s), rs)
                            elif m == "Ca":
                                PR.wait("scalar", mtc, mm_t)
                                sct = PR.op("scalar", lambda e, o=sg_f[:, sl], s=bank[:, 0:512]:
                                            e.activation(out=o, in_=s, func=AF.Sigmoid), inc=msc)
                                PR.wait("vector", msc, sct)
                                rs = ringF.plan_release()
                                PR.op_preinc("vector", lambda e, o=sil_f[:, sl], s=bank[:, 0:512],
                                      g2=sg_f[:, sl]: e.tensor_mul(out=o, in0=g2, in1=s), rs)
                            else:
                                PR.wait("vector", mtc, mm_t)
                                rs = ringF.plan_release()
                                PR.op_preinc("vector", lambda e, o=c_b[:, sl], s=bank[:, 0:512],
                                      g2=sil_f[:, sl]: e.tensor_mul(out=o, in0=g2, in1=s), rs)
                if ng >= 2:
                    PR.wait("vector", dgp, ywr[ng - 2])

                def _scan(v):
                    v.tensor_tensor_scan(out=h_f, data0=a_f, data1=u_f, initial=0.0,
                                         op0=ALU.mult, op1=ALU.add)
                    return v.tensor_mul(out=y_b, in0=c_b, in1=h_f)
                yt = PR.op("vector", _scan, inc=ngd)
                PR.wait("gpsimd", ngd, yt)
                PR.op("gpsimd", lambda e, d=y_scr[ng], s=y_b: e.dma_start(out=d, in_=s),
                      inc=dgp, k=16)
                ywr.append(dgp.n)
                if pump:
                    pump(2)
            PR.barrier()
            if dbg_on and b == 0:
                PR.op("sync", lambda e: e.dma_start(out=dbg2, in_=y_scr), inc=dsy, k=16)
                PR.wait("sync", dsy, dsy.n)
            A.release()
            if "3" not in a_sub:
                return

            A.mark()
            outb_sb, outb_t = load_row(outb_bc)
            PR.wait("vector", dsy, outb_t)
            outw_sb = A.alloc(KT * D, BF16)
            for kt in range(KT):
                PR.op("gpsimd", lambda e, d=outw_sb[:, kt * D:(kt + 1) * D], s=rhs_outw[kt]:
                      e.dma_start(out=d, in_=s), inc=dgp, k=16)
            owt = dgp.n
            PR.wait("tensor", dgp, owt)
            xt = [A.alloc(D, F32) for _ in range(2)]
            yb = [A.alloc(D, BF16) for _ in range(2)]
            r_f = A.alloc(D, F32)
            xnew = [A.alloc(D, F32) for _ in range(2)]
            c1a = A.alloc(SW, F32)
            c2a = A.alloc(SW, F32)

            def _pre_all(v):
                v.tensor_scalar_add(out=c1a, in0=stats[:, 16:16 + SW], scalar1=1.0)
                v.tensor_mul(out=c2a, in0=stats[:, 0:SW], in1=stats[:, 16:16 + SW])
                return v.tensor_scalar_mul(out=c2a, in0=c2a, scalar1=-1.0)
            PR.op("vector", _pre_all, inc=mvc)
            pre_t, store_t, ybl_t = [], [], []
            for tt in range(16):
                g = b * 16 + tt
                buf = tt % 2
                if tt >= 2:
                    PR.wait("sync", mvc, pre_t[tt - 2])
                xl = PR.op("sync", lambda e, d=xt[buf], s=x_in[g]:
                           e.dma_start(out=d, in_=s), inc=dsy, k=16)
                if tt >= 2:
                    PR.wait("gpsimd", mtc, ybl_t[tt - 2])
                ybl = PR.op("gpsimd", lambda e, d=yb[buf].rearrange("p (n t) -> p n t", n=KT),
                            s=y_scr[:, :, tt * P:(tt + 1) * P].rearrange("n p t -> p n t"):
                            e.dma_start(out=d, in_=s), inc=dgp, k=16)
                PR.wait("tensor", dgp, ybl)
                PR.wait("vector", dsy, xl)
                if tt >= 2:
                    PR.wait("vector", dsy, store_t[tt - 2])

                def _pre(v, X=xt[buf], tt=tt):
                    v.tensor_scalar(out=r_f, in0=X, scalar1=c1a[:, tt:tt + 1],
                                    scalar2=c2a[:, tt:tt + 1], op0=ALU.mult, op1=ALU.add)
                    return v.tensor_add(out=r_f, in0=r_f, in1=outb_sb)
                pre_t.append(PR.op("vector", _pre, inc=mvc))
                for sr in range(2):
                    bks = [ringF.acquire(), ringF.acquire()]
                    for kt in range(KT):
                        for half in range(2):
                            dgr = sr * 2 + half
                            PR.op("tensor", lambda e, o=bks[half][:, 0:512],
                                  l=yb[buf][:, kt * P:(kt + 1) * P],
                                  r=outw_sb[:, kt * D + dgr * 512:kt * D + (dgr + 1) * 512],
                                  st=(kt == 0), sp=(kt == KT - 1):
                                  e.matmul(o, l, r, start=st, stop=sp),
                                  inc=mtc if (kt == KT - 1 and half == 1) else None)
                    mm_t = mtc.n
                    PR.wait("vector", mtc, mm_t)
                    for half in range(2):
                        dgr = sr * 2 + half
                        sl = slice(dgr * 512, (dgr + 1) * 512)
                        rs = ringF.plan_release()
                        last = (sr == 1 and half == 1)
                        PR.op_preinc("vector", lambda e, o=xnew[buf][:, sl], a1=r_f[:, sl],
                              s=bks[half][:, 0:512]: e.tensor_add(out=o, in0=a1, in1=s), rs)
                ybl_t.append(mtc.n)
                vend = PR.op("vector", lambda e: e.engine_nop(), inc=mvc)
                PR.wait("sync", mvc, vend)
                PR.op("sync", lambda e, d=x_scr[g], s=xnew[buf]: e.dma_start(out=d, in_=s),
                      inc=dsy, k=16)
                store_t.append(dsy.n)
                if pump:
                    pump(4)
            if pump:
                Wst["done_t"] = drain()
            PR.barrier()
            A.release()
            if with_W:
                A.release()

        def phase_ffn(b, x_src, x_dst, tag):
            if Wst.get("done_t"):
                PR.wait("sync", dgp, Wst["done_t"])
                PR.wait("gpsimd", dgp, Wst["done_t"])
            for ch in range(2):
                A.mark()
                h_b = A.alloc(IT * FCH, BF16)
                A.mark()
                xnT = A.alloc(KT * FCH, BF16)
                tiles = list(range(ch * 8, ch * 8 + 8))
                ln_transpose(b, x_src, xnT, tiles, rms_mode=False,
                             save_stats=False, tag=f"f{b}{ch}")
                PR.barrier()
                wgu = [[A.alloc(KT * P, BF16) for _ in range(2)] for _ in range(2)]
                sg_f = A.alloc(FCH, F32)
                sil_f = A.alloc(FCH, F32)
                itd = PR.sem(f"ffn{tag}{b}{ch}it")
                for it in range(IT):
                    buf = it % 2
                    if it >= 2:
                        PR.wait("sync", itd, it - 1)
                    for gi, scr in ((0, wg_scr), (1, wu_scr)):
                        PR.op("sync", lambda e,
                              d=wgu[gi][buf].rearrange("p (kt m) -> p kt m", kt=KT),
                              s=scr[it].rearrange("kt p m -> p kt m"):
                              e.dma_start(out=d, in_=s), inc=dsy, k=16)
                    wt = dsy.n
                    PR.wait("tensor", dsy, wt)
                    jobs = []
                    for gi in range(2):
                        bks = [ringF.acquire(), ringF.acquire()]
                        for kt in range(KT):
                            for half in range(2):
                                PR.op("tensor", lambda e, o=bks[half][:, 0:512],
                                      l=wgu[gi][buf][:, kt * P:(kt + 1) * P],
                                      r=xnT[:, kt * FCH + half * 512:kt * FCH + (half + 1) * 512],
                                      st=(kt == 0), sp=(kt == KT - 1):
                                      e.matmul(o, l, r, start=st, stop=sp),
                                      inc=mtc if (kt == KT - 1 and half == 1) else None)
                        jobs.append((bks, mtc.n))
                    (gb, gt), (ub, ut) = jobs
                    PR.wait("scalar", mtc, gt)
                    for half in range(2):
                        sl = slice(half * 512, (half + 1) * 512)
                        sct = PR.op("scalar", lambda e, o=sg_f[:, sl], s=gb[half][:, 0:512]:
                                    e.activation(out=o, in_=s, func=AF.Sigmoid), inc=msc)
                        PR.wait("vector", msc, sct)
                        rs = ringF.plan_release()
                        PR.op_preinc("vector", lambda e, o=sil_f[:, sl], g2=sg_f[:, sl],
                              s=gb[half][:, 0:512]: e.tensor_mul(out=o, in0=g2, in1=s), rs)
                    PR.wait("vector", mtc, ut)
                    for half in range(2):
                        sl = slice(half * 512, (half + 1) * 512)
                        rs = ringF.plan_release()
                        PR.op_preinc("vector", lambda e,
                              o=h_b[:, it * FCH + half * 512:it * FCH + (half + 1) * 512],
                              g2=sil_f[:, sl], s=ub[half][:, 0:512]:
                              e.tensor_mul(out=o, in0=g2, in1=s), rs)
                    PR.op("vector", lambda e: e.engine_nop(), inc=itd)

                PR.wait("tensor", itd, IT)
                PR.barrier()
                A.release()
                wd_sb = A.alloc(IT * 512, BF16)
                xsl = [A.alloc(512, F32) for _ in range(2)]
                osl = [A.alloc(512, F32) for _ in range(2)]
                xsl_t, osl_t = {}, {}
                prev_mm = 0
                for dg in range(4):
                    if dg > 0:
                        PR.wait("gpsimd", mtc, prev_mm)
                    PR.op("gpsimd", lambda e, d=wd_sb.rearrange("p (it c) -> p it c", it=IT),
                          s=wd_scr[dg].rearrange("it p c -> p it c"):
                          e.dma_start(out=d, in_=s), inc=dgp, k=16)
                    slab = dgp.n
                    PR.wait("tensor", dgp, slab)
                    for tsub in range(8):
                        i = dg * 8 + tsub
                        buf = i % 2
                        g = b * 16 + ch * 8 + tsub
                        if i >= 2:
                            PR.wait("sync", mvc, osl_t[i - 2])
                        xst = PR.op("sync", lambda e, d=xsl[buf],
                                    s=x_src[g][:, dg * 512:(dg + 1) * 512]:
                                    e.dma_start(out=d, in_=s), inc=dsy, k=16)
                        xsl_t[i] = xst
                        bank = ringF.acquire()
                        for ki in range(IT):
                            PR.op("tensor", lambda e, o=bank[:, 0:512],
                                  l=h_b[:, ki * FCH + tsub * P:ki * FCH + (tsub + 1) * P],
                                  r=wd_sb[:, ki * 512:(ki + 1) * 512],
                                  st=(ki == 0), sp=(ki == IT - 1):
                                  e.matmul(o, l, r, start=st, stop=sp),
                                  inc=mtc if ki == IT - 1 else None)
                        mm_t = mtc.n
                        PR.wait("vector", mtc, mm_t)
                        PR.wait("vector", dsy, xst)
                        if i >= 2:
                            PR.wait("vector", dsy, osl_t.get(("st", i - 2), 0))
                        rs = ringF.plan_release()
                        ot = PR.op_preinc("vector", lambda e, o=osl[buf], X=xsl[buf],
                                   s=bank[:, 0:512]: e.tensor_add(out=o, in0=X, in1=s), rs)
                        ve = PR.op("vector", lambda e: e.engine_nop(), inc=mvc)
                        osl_t[i] = ve
                        PR.wait("sync", mvc, ve)
                        PR.op("sync", lambda e, d=x_dst[g][:, dg * 512:(dg + 1) * 512],
                              s=osl[buf]: e.dma_start(out=d, in_=s), inc=dsy, k=16)
                        osl_t[("st", i)] = dsy.n
                    prev_mm = mtc.n
                PR.barrier()
                A.release()

        def phase_C(b):
            A.mark()
            xnT = A.alloc(KT * T, BF16)
            xcT = A.alloc(KT * T, BF16)
            A.mark()
            ln_transpose(b, x_scr, xnT, list(range(16)), rms_mode=False,
                         save_stats=False, tag=f"c1_{b}")
            PR.barrier()
            A.release()

            A.mark()
            pad = A.alloc(T + 4, BF16)
            accv = A.alloc(T, F32)
            sgc = A.alloc(T, F32)
            for ct in range(16):
                def _vc(v, ct=ct):
                    v.memset(pad[:, 0:4], 0.0)
                    v.tensor_copy(out=pad[:, 4:4 + T], in_=xnT[:, ct * T:(ct + 1) * T])
                    v.tensor_scalar(out=accv, in0=pad[:, 1:1 + T],
                                    scalar1=convw_sb[:, ct * 4:ct * 4 + 1],
                                    scalar2=convb_sb[:, ct:ct + 1], op0=ALU.mult, op1=ALU.add)
                    last = None
                    for j2 in range(1, 4):
                        last = v.scalar_tensor_tensor(out=accv, in0=pad[:, 1 + j2:1 + j2 + T],
                                               scalar=convw_sb[:, ct * 4 + j2:ct * 4 + j2 + 1],
                                               in1=accv, op0=ALU.mult, op1=ALU.add)
                    return last
                vt = PR.op("vector", _vc, inc=mvc)
                PR.wait("scalar", mvc, vt)
                sct = PR.op("scalar", lambda e: e.activation(out=sgc, in_=accv, func=AF.Sigmoid),
                            inc=msc)
                PR.wait("vector", msc, sct)
                PR.op("vector", lambda e, o=xcT[:, ct * T:(ct + 1) * T]:
                      e.tensor_mul(out=o, in0=accv, in1=sgc), inc=mvc)
            PR.barrier()
            A.release()

            A.mark()
            vng_sb, t1 = load_row(vng_bc)
            vnb_sb, t2 = load_row(vnb_bc)
            PR.wait("vector", dsy, t2)
            wsl = [A.alloc(KT * 512, BF16) for _ in range(2)]
            pf = [A.alloc(512, F32) for _ in range(2)]
            sqf = [A.alloc(512, F32) for _ in range(2)]
            ssb = A.alloc(8, F32)
            outt = [A.alloc(512, BF16) for _ in range(2)]
            tps = [A.alloc(512, BF16) for _ in range(2)]
            for mat, rhs_w, scr in (("q", rhs_q, None), ("k", rhs_k, k_scr),
                                    ("v", rhs_v, v_scr)):
                vn_t, out_t = {}, {}
                for cg in range(4):
                    wb = cg % 2
                    if cg >= 2:
                        PR.wait("gpsimd", mtc, out_t.get(("mmlast", cg - 2), 0))
                    PR.op("gpsimd", lambda e, d=wsl[wb].rearrange("p (kt c) -> p kt c", kt=KT),
                          s=rhs_w[:, :, cg * 512:(cg + 1) * 512].rearrange("kt p c -> p kt c"):
                          e.dma_start(out=d, in_=s), inc=dgp, k=16)
                    wt = dgp.n
                    PR.wait("tensor", dgp, wt)
                    for tt in range(16):
                        i = cg * 16 + tt
                        buf = i % 2
                        bank = ringF.acquire()
                        for kt in range(KT):
                            PR.op("tensor", lambda e, o=bank[:, 0:512],
                                  l=xnT[:, kt * T + tt * P:kt * T + (tt + 1) * P],
                                  r=wsl[wb][:, kt * 512:(kt + 1) * 512],
                                  st=(kt == 0), sp=(kt == KT - 1):
                                  e.matmul(o, l, r, start=st, stop=sp),
                                  inc=mtc if kt == KT - 1 else None)
                        mm_t = mtc.n
                        out_t[("mmlast", cg)] = mm_t
                        PR.wait("scalar", mtc, mm_t)
                        if i >= 2:
                            PR.wait("scalar", mvc, vn_t[i - 2])
                        PR.op("scalar", lambda e, o=pf[buf], s=bank[:, 0:512]:
                              e.copy(out=o, in_=s), inc=msc)
                        rs = ringF.plan_release()
                        PR.op_preinc("scalar", lambda e, o=sqf[buf], s=bank[:, 0:512]:
                              e.activation(out=o, in_=s, func=AF.Square), rs)
                        PR.wait("vector", rs, rs.n)
                        if i >= 2:
                            if mat == "q":
                                PR.wait("vector", dgp, out_t.get(i - 2, 0))
                            else:
                                PR.wait("vector", mtc, out_t.get(("tp", i - 2), 0))

                        ftv = mvc.n + 1

                        def _vn(v, mat=mat, cg=cg, bf=buf, ftv=ftv):
                            PF, SQ, OT = pf[bf], sqf[bf], outt[bf]
                            if mat in ("q", "k"):
                                for hh in range(4):
                                    r2 = v.reduce_sum(out=s_m[:, hh:hh + 1],
                                                      in_=SQ[:, hh * P:(hh + 1) * P], axis=AXX)
                                    if hh == 3:
                                        r2.then_inc(mvc.h, 1)
                                v.wait_ge(mvc.h, ftv)
                                v.tensor_scalar_max(out=s_m, in0=s_m, scalar1=1e-24)
                                rsqrt_emit(v, s_r, s_m)
                                last = None
                                for hh in range(4):
                                    last = v.tensor_scalar(out=OT[:, hh * P:(hh + 1) * P],
                                                           in0=PF[:, hh * P:(hh + 1) * P],
                                                           scalar1=s_r[:, hh:hh + 1],
                                                           scalar2=None, op0=ALU.mult)
                                return last
                            else:
                                for hh in range(4):
                                    v.reduce_sum(out=s_m[:, hh:hh + 1],
                                                 in_=PF[:, hh * P:(hh + 1) * P], axis=AXX)
                                for hh in range(4):
                                    r2 = v.reduce_sum(out=s_v[:, hh:hh + 1],
                                                      in_=SQ[:, hh * P:(hh + 1) * P], axis=AXX)
                                    if hh == 3:
                                        r2.then_inc(mvc.h, 1)
                                v.wait_ge(mvc.h, ftv)
                                v.tensor_scalar_mul(out=s_m, in0=s_m, scalar1=1.0 / P)
                                v.tensor_scalar_mul(out=s_v, in0=s_v, scalar1=1.0 / P)
                                v.tensor_mul(out=s_t1, in0=s_m, in1=s_m)
                                v.tensor_sub(out=s_v, in0=s_v, in1=s_t1)
                                v.tensor_scalar_add(out=s_t1, in0=s_v, scalar1=EPS)
                                rsqrt_emit(v, s_r, s_t1)
                                v.tensor_mul(out=s_nm, in0=s_m, in1=s_r)
                                v.tensor_scalar_mul(out=s_nm, in0=s_nm, scalar1=-1.0)
                                for hh in range(4):
                                    v.tensor_scalar(out=OT[:, hh * P:(hh + 1) * P],
                                                    in0=PF[:, hh * P:(hh + 1) * P],
                                                    scalar1=s_r[:, hh:hh + 1],
                                                    scalar2=s_nm[:, hh:hh + 1],
                                                    op0=ALU.mult, op1=ALU.add)
                                sl = slice(cg * 512, (cg + 1) * 512)
                                v.tensor_mul(out=OT, in0=OT, in1=vng_sb[:, sl])
                                return v.tensor_add(out=OT, in0=OT, in1=vnb_sb[:, sl])
                        mvc.n += 1
                        vn_t[i] = PR.op("vector", _vn, inc=mvc)
                        if mat == "q":
                            PR.wait("gpsimd", mvc, vn_t[i])
                            PR.op("gpsimd", lambda e, d=q_scr[tt][:, cg * 512:(cg + 1) * 512],
                                  s=outt[buf]: e.dma_start(out=d, in_=s), inc=dgp, k=16)
                            out_t[i] = dgp.n
                        else:
                            PR.wait("tensor", mvc, vn_t[i])
                            bkb = ringB.acquire()
                            for hh in range(4):
                                PR.op("tensor", lambda e, o=bkb[:, hh * P:(hh + 1) * P],
                                      i2=outt[buf][:, hh * P:(hh + 1) * P]:
                                      e.transpose(o, i2, ident_b),
                                      inc=mtc if hh == 3 else None)
                            tpt = mtc.n
                            out_t[("tp", i)] = tpt
                            PR.wait("scalar", mtc, tpt)
                            if i >= 2:
                                PR.wait("scalar", dgp, out_t.get(i - 2, 0))
                            rsb = ringB.plan_release()
                            PR.op_preinc("scalar", lambda e, d=tps[buf], s=bkb[:, 0:512]:
                                  e.copy(out=d, in_=s), rsb)
                            PR.wait("gpsimd", rsb, rsb.n)
                            for hh in range(4):
                                ct = cg * 4 + hh
                                PR.op("gpsimd", lambda e, d=scr[ct, tt],
                                      s=tps[buf][:, hh * P:(hh + 1) * P]:
                                      e.dma_start(out=d, in_=s), inc=dgp, k=16)
                            out_t[i] = dgp.n
                PR.wait("vector", dgp, dgp.n)
            PR.barrier()
            A.release()

            A.mark()
            ogb_sb, togb = load_row(ogb_bc)
            PR.wait("vector", dsy, togb)
            igw = [A.alloc(KT * P, BF16) for _ in range(2)]
            igx = A.alloc(T, BF16)
            wsl4 = [A.alloc(KT * 512, BF16) for _ in range(2)]
            ig_wr = {}
            for nt in range(16):
                buf = nt % 2
                if nt >= 2:
                    PR.wait("gpsimd", mtc, ig_wr[("mm", nt - 2)])
                PR.op("gpsimd", lambda e, d=igw[buf].rearrange("p (kt m) -> p kt m", kt=KT),
                      s=lhs_ig[nt].rearrange("kt p m -> p kt m"):
                      e.dma_start(out=d, in_=s), inc=dgp, k=16)
                wt = dgp.n
                PR.wait("tensor", dgp, wt)
                if nt >= 1:
                    PR.wait("scalar", dgp, ig_wr[nt - 1])
                for ts in range(4):
                    bank = ringF.acquire()
                    for kt in range(KT):
                        PR.op("tensor", lambda e, o=bank[:, 0:512],
                              l=igw[buf][:, kt * P:(kt + 1) * P],
                              r=xcT[:, kt * T + ts * 512:kt * T + (ts + 1) * 512],
                              st=(kt == 0), sp=(kt == KT - 1):
                              e.matmul(o, l, r, start=st, stop=sp),
                              inc=mtc if kt == KT - 1 else None)
                    mm_t = mtc.n
                    PR.wait("scalar", mtc, mm_t)
                    rs = ringF.plan_release()
                    PR.op_preinc("scalar", lambda e, o=igx[:, ts * 512:(ts + 1) * 512],
                          s=bank[:, 0:512], nn=nt:
                          e.activation(out=o, in_=s, func=AF.Sigmoid,
                                       bias=igb_sb[:, nn:nn + 1]), rs)
                ig_wr[("mm", nt)] = mtc.n
                sce = PR.op("scalar", lambda e: e.copy(out=gscr[:, 6:7], in_=gscr[:, 6:7]),
                            inc=msc)
                PR.wait("gpsimd", msc, sce)
                for half in range(2):
                    PR.op("gpsimd", lambda e,
                          d=ig_scr[nt, half * 8:(half + 1) * 8].rearrange("t p c -> p t c"),
                          s=igx.rearrange("p (t c) -> p t c", t=KT)[:, half * 8:(half + 1) * 8, :]:
                          e.dma_start(out=d, in_=s), inc=dgp, k=16)
                ig_wr[nt] = dgp.n
            ogx = [A.alloc(512, F32) for _ in range(2)]
            ogo = [A.alloc(512, BF16) for _ in range(2)]
            og_t = {}
            for cg in range(4):
                wb = cg % 2
                if cg >= 2:
                    PR.wait("gpsimd", mtc, og_t[("mm", cg - 2)])
                PR.op("gpsimd", lambda e, d=wsl4[wb].rearrange("p (kt c) -> p kt c", kt=KT),
                      s=rhs_og[:, :, cg * 512:(cg + 1) * 512].rearrange("kt p c -> p kt c"):
                      e.dma_start(out=d, in_=s), inc=dgp, k=16)
                wt = dgp.n
                PR.wait("tensor", dgp, wt)
                for tt in range(16):
                    i = cg * 16 + tt
                    buf = i % 2
                    bank = ringF.acquire()
                    for kt in range(KT):
                        PR.op("tensor", lambda e, o=bank[:, 0:512],
                              l=xcT[:, kt * T + tt * P:kt * T + (tt + 1) * P],
                              r=wsl4[wb][:, kt * 512:(kt + 1) * 512],
                              st=(kt == 0), sp=(kt == KT - 1):
                              e.matmul(o, l, r, start=st, stop=sp),
                              inc=mtc if kt == KT - 1 else None)
                    mm_t = mtc.n
                    og_t[("mm", cg)] = mm_t
                    PR.wait("vector", mtc, mm_t)
                    if i >= 2:
                        PR.wait("vector", msc, og_t[("sg", i - 2)])
                    rs = ringF.plan_release()
                    PR.op_preinc("vector", lambda e, o=ogx[buf], s=bank[:, 0:512],
                          bb=ogb_sb[:, cg * 512:(cg + 1) * 512]:
                          e.tensor_add(out=o, in0=s, in1=bb), rs)
                    vxt = mvc.n
                    if i >= 2:
                        PR.wait("scalar", dgp, og_t[i - 2])
                    PR.wait("scalar", ringF.rel, rs.n)
                    sgt = PR.op("scalar", lambda e, o=ogo[buf], s=ogx[buf]:
                                e.activation(out=o, in_=s, func=AF.Sigmoid), inc=msc)
                    og_t[("sg", i)] = sgt
                    PR.wait("gpsimd", msc, sgt)
                    PR.op("gpsimd", lambda e, d=og_scr[tt][:, cg * 512:(cg + 1) * 512],
                          s=ogo[buf]: e.dma_start(out=d, in_=s), inc=dgp, k=16)
                    og_t[i] = dgp.n
            gwb = A.alloc(KT * H, BF16)
            gamT = A.alloc(T, F32)
            PR.op("gpsimd", lambda e, d=gwb.rearrange("p (kt h) -> p kt h", kt=KT),
                  s=lhs_gw.rearrange("kt p h -> p kt h"):
                  e.dma_start(out=d, in_=s), inc=dgp, k=16)
            gwt = dgp.n
            PR.wait("tensor", dgp, gwt)
            for ts in range(4):
                bank = ringF.acquire()
                for kt in range(KT):
                    PR.op("tensor", lambda e, o=bank[0:H, 0:512],
                          l=gwb[:, kt * H:(kt + 1) * H],
                          r=xcT[:, kt * T + ts * 512:kt * T + (ts + 1) * 512],
                          st=(kt == 0), sp=(kt == KT - 1):
                          e.matmul(o, l, r, start=st, stop=sp),
                          inc=mtc if kt == KT - 1 else None)
                mm_t = mtc.n
                PR.wait("scalar", mtc, mm_t)
                rs = ringF.plan_release()
                PR.op_preinc("scalar", lambda e, o=gamT[0:H, ts * 512:(ts + 1) * 512],
                      s=bank[0:H, 0:512]:
                      e.activation(out=o, in_=s, func=AF.Sigmoid,
                                   bias=gammab_sb[0:16, :]), rs)
            gdone = PR.op("scalar", lambda e: e.copy(out=gscr[0:16, 7:8], in_=gscr[0:16, 7:8]),
                          inc=msc)
            PR.wait("gpsimd", msc, gdone)
            PR.op("gpsimd", lambda e: e.dma_start(out=gam_scr, in_=gamT[0:H, :]),
                  inc=dgp, k=16)
            PR.barrier()
            A.release()
            A.release()

            A.mark()
            kch = [A.alloc(16 * 256, BF16) for _ in range(2)]
            vch = [A.alloc(16 * 256, BF16) for _ in range(2)]
            ich = [A.alloc(16 * 256, BF16) for _ in range(2)]
            ach = [A.alloc(16 * 256, F32) for _ in range(2)]
            b1 = A.alloc(256, F32)
            memc = A.alloc(256, F32)
            mtok = [A.alloc(D, F32) for _ in range(2)]
            scan_t = {}
            mtok_t = {}
            mst_t = {}
            for ci in range(8):
                cb = ci % 2
                if ci >= 2:
                    PR.wait("sync", mvc, scan_t[ci - 2])
                for ct in range(16):
                    for dst_buf, scr2 in ((kch[cb], k_scr), (vch[cb], v_scr), (ich[cb], ig_scr)):
                        PR.op("sync", lambda e,
                              d=dst_buf[:, ct * 256:(ct + 1) * 256].rearrange("p (t c) -> p t c", t=2),
                              s=scr2[ct, 2 * ci:2 * ci + 2].rearrange("t p c -> p t c"):
                              e.dma_start(out=d, in_=s), inc=dsy, k=16)
                    PR.op("sync", lambda e, d=ach[cb][:, ct * 256:(ct + 1) * 256],
                          s=gam_scr[ct:ct + 1, ci * 256:(ci + 1) * 256].partition_broadcast(P):
                          e.dma_start(out=d, in_=s), inc=dsy, k=16)
                kvit = dsy.n
                PR.wait("vector", dsy, kvit)
                for ct in range(16):
                    ft2 = mvc.n + 1

                    def _vb(v, cc=ct, cb=cb, ci=ci, ft2=ft2):
                        sl = slice(cc * 256, (cc + 1) * 256)
                        v.tensor_mul(out=b1, in0=ich[cb][:, sl], in1=kch[cb][:, sl])
                        v.tensor_mul(out=b1, in0=b1, in1=vch[cb][:, sl])
                        if ci == 0:
                            sc2 = v.tensor_tensor_scan(out=memc, data0=ach[cb][:, sl], data1=b1,
                                                       initial=0.0, op0=ALU.mult, op1=ALU.add)
                        else:
                            sc2 = v.tensor_tensor_scan(out=memc, data0=ach[cb][:, sl], data1=b1,
                                                       initial=carryB[:, cc:cc + 1],
                                                       op0=ALU.mult, op1=ALU.add)
                        sc2.then_inc(mvc.h, 1)
                        v.wait_ge(mvc.h, ft2)
                        return v.tensor_copy(out=carryB[:, cc:cc + 1], in_=memc[:, 255:256])
                    mvc.n += 1
                    if ct >= 1:
                        PR.wait("vector", mtc, mtok_t[("tpct", ci, ct - 1)])
                    elif ci >= 1:
                        PR.wait("vector", mtc, mtok_t.get(("tp", ci - 1), 0))
                    vbt = PR.op("vector", _vb, inc=mvc)
                    PR.wait("tensor", mvc, vbt)
                    for j in range(2):
                        bank = ringF.acquire()
                        PR.op("tensor", lambda e, o=bank[:, 0:P],
                              i2=memc[:, j * P:(j + 1) * P]:
                              e.transpose(o, i2, ident_f), inc=mtc)
                        tpt = mtc.n
                        mtok_t[("tpct", ci, ct)] = tpt
                        PR.wait("scalar", mtc, tpt)
                        if ct == 0 and j == 0 and ci >= 2:
                            PR.wait("scalar", dsy, mst_t.get(ci - 2, 0))
                        rs = ringF.plan_release()
                        PR.op_preinc("scalar", lambda e, d=mtok[j][:, ct * P:(ct + 1) * P],
                              s=bank[:, 0:P]: e.copy(out=d, in_=s), rs)
                mtok_t[("tp", ci)] = mtc.n
                scan_t[ci] = PR.op("vector", lambda e: e.engine_nop(), inc=mvc)
                mcopy_t = PR.op("scalar", lambda e: e.copy(out=dumm[:, 0:1],
                                in_=dumm[:, 0:1]), inc=msc)
                PR.wait("sync", msc, mcopy_t)
                for j in range(2):
                    PR.op("sync", lambda e, d=mem_scr[ci * 2 + j], s=mtok[j]:
                          e.dma_start(out=d, in_=s), inc=dsy, k=16)
                mst_t[ci] = dsy.n
            PR.barrier()
            if dbg_on and b == 0:
                PR.op("sync", lambda e: e.dma_start(out=dbg3, in_=mem_scr), inc=dsy, k=16)
                PR.wait("sync", dsy, dsy.n)
            A.release()

            A.mark()
            mng_sb, _t1 = load_row(mng_bc)
            mnb_sb, _t2 = load_row(mnb_bc)
            gng_sb, _t3 = load_row(gng_bc)
            gnb_sb, gn_t = load_row(gnb_bc)
            PR.wait("vector", dsy, gn_t)
            wo_sb = A.alloc(KT * D, BF16)
            for kt in range(KT):
                PR.op("gpsimd", lambda e, d=wo_sb[:, kt * D:(kt + 1) * D], s=rhs_wo[kt]:
                      e.dma_start(out=d, in_=s), inc=dgp, k=16)
            wot = dgp.n
            PR.wait("tensor", dgp, wot)
            mt5 = [A.alloc(D, F32) for _ in range(2)]
            sq5 = [A.alloc(D, F32) for _ in range(2)]
            vq = A.alloc(D, F32)
            qt = [A.alloc(D, BF16) for _ in range(2)]
            ogt = [A.alloc(D, BF16) for _ in range(2)]
            xt5 = [A.alloc(D, F32) for _ in range(2)]
            o1 = A.alloc(D, F32)
            obf = A.alloc(D, BF16)
            oTs = A.alloc(D, BF16)
            xnw = A.alloc(D, F32)
            vo_t, wo_t, st5_t, otp_t = {}, {}, {}, {}
            for tt in range(16):
                g = b * 16 + tt
                buf = tt % 2
                if tt >= 2:
                    PR.wait("sync", mvc, vo_t[tt - 2])
                for dd, ss in ((mt5[buf], mem_scr[tt]), (qt[buf], q_scr[tt]),
                               (ogt[buf], og_scr[tt]), (xt5[buf], x_scr[g])):
                    PR.op("sync", lambda e, d=dd, s=ss: e.dma_start(out=d, in_=s),
                          inc=dsy, k=16)
                ld5 = dsy.n
                PR.wait("vector", dsy, ld5)
                if tt >= 1:
                    PR.wait("vector", mtc, otp_t.get(tt - 1, 0))

                PR.wait("scalar", dsy, ld5)
                if tt >= 2:
                    PR.wait("scalar", mvc, vo_t[tt - 2])
                sq5t = PR.op("scalar", lambda e, d=sq5[buf], s=mt5[buf]:
                             e.activation(out=d, in_=s, func=AF.Square), inc=msc)
                ftv1 = mvc.n + 1
                ftv2 = mvc.n + 2

                def _vo(v, buf=buf, sq5t=sq5t, ftv1=ftv1, ftv2=ftv2):
                    M = mt5[buf]
                    v.wait_ge(msc.h, sq5t)
                    for hh in range(16):
                        v.reduce_sum(out=s_m[:, hh:hh + 1], in_=M[:, hh * P:(hh + 1) * P],
                                     axis=AXX)
                    for hh in range(16):
                        r2 = v.reduce_sum(out=s_v[:, hh:hh + 1],
                                          in_=sq5[buf][:, hh * P:(hh + 1) * P], axis=AXX)
                        if hh == 15:
                            r2.then_inc(mvc.h, 1)
                    v.wait_ge(mvc.h, ftv1)
                    v.tensor_scalar_mul(out=s_m, in0=s_m, scalar1=1.0 / P)
                    v.tensor_scalar_mul(out=s_v, in0=s_v, scalar1=1.0 / P)
                    v.tensor_mul(out=s_t1, in0=s_m, in1=s_m)
                    v.tensor_sub(out=s_v, in0=s_v, in1=s_t1)
                    v.tensor_scalar_add(out=s_t1, in0=s_v, scalar1=EPS)
                    rsqrt_emit(v, s_r, s_t1)
                    v.tensor_mul(out=s_nm, in0=s_m, in1=s_r)
                    v.tensor_scalar_mul(out=s_nm, in0=s_nm, scalar1=-1.0)
                    for hh in range(16):
                        sl = slice(hh * P, (hh + 1) * P)
                        v.tensor_scalar(out=o1[:, sl], in0=M[:, sl],
                                        scalar1=s_r[:, hh:hh + 1], scalar2=s_nm[:, hh:hh + 1],
                                        op0=ALU.mult, op1=ALU.add)
                    v.tensor_mul(out=o1, in0=o1, in1=mng_sb)
                    v.tensor_add(out=o1, in0=o1, in1=mnb_sb)
                    v.tensor_mul(out=o1, in0=o1, in1=qt[buf])
                    v.tensor_mul(out=vq, in0=o1, in1=o1)
                    for hh in range(16):
                        v.reduce_sum(out=s_m[:, hh:hh + 1], in_=o1[:, hh * P:(hh + 1) * P],
                                     axis=AXX)
                    for hh in range(16):
                        r2 = v.reduce_sum(out=s_v[:, hh:hh + 1],
                                          in_=vq[:, hh * P:(hh + 1) * P], axis=AXX)
                        if hh == 15:
                            r2.then_inc(mvc.h, 1)
                    v.wait_ge(mvc.h, ftv2)
                    v.tensor_scalar_mul(out=s_m, in0=s_m, scalar1=1.0 / P)
                    v.tensor_scalar_mul(out=s_v, in0=s_v, scalar1=1.0 / P)
                    v.tensor_mul(out=s_t1, in0=s_m, in1=s_m)
                    v.tensor_sub(out=s_v, in0=s_v, in1=s_t1)
                    v.tensor_scalar_add(out=s_t1, in0=s_v, scalar1=EPS)
                    rsqrt_emit(v, s_r, s_t1)
                    v.tensor_mul(out=s_nm, in0=s_m, in1=s_r)
                    v.tensor_scalar_mul(out=s_nm, in0=s_nm, scalar1=-1.0)
                    for hh in range(16):
                        sl = slice(hh * P, (hh + 1) * P)
                        v.tensor_scalar(out=o1[:, sl], in0=o1[:, sl],
                                        scalar1=s_r[:, hh:hh + 1], scalar2=s_nm[:, hh:hh + 1],
                                        op0=ALU.mult, op1=ALU.add)
                    v.tensor_mul(out=o1, in0=o1, in1=gng_sb)
                    v.tensor_add(out=o1, in0=o1, in1=gnb_sb)
                    return v.tensor_mul(out=obf, in0=o1, in1=ogt[buf])
                mvc.n += 2
                vo_t[tt] = PR.op("vector", _vo, inc=mvc)
                PR.wait("tensor", mvc, vo_t[tt])
                for half in range(2):
                    bkb = ringB.acquire()
                    for jj in range(8):
                        ct = half * 8 + jj
                        PR.op("tensor", lambda e, o=bkb[:, jj * P:(jj + 1) * P],
                              i2=obf[:, ct * P:(ct + 1) * P]:
                              e.transpose(o, i2, ident_b),
                              inc=mtc if jj == 7 else None)
                    tpt = mtc.n
                    PR.wait("scalar", mtc, tpt)
                    if half == 0 and tt >= 1:
                        PR.wait("scalar", mtc, wo_t.get(tt - 1, 0))
                    rsb = ringB.plan_release()
                    PR.op_preinc("scalar", lambda e,
                          d=oTs.rearrange("p (k t) -> p k t", k=KT)[:, half * 8:half * 8 + 8, :],
                          s=bkb.rearrange("p (k t) -> p k t", k=8):
                          e.copy(out=d, in_=s), rsb)
                otp_t[tt] = mtc.n
                oT_rdy = PR.op("scalar", lambda e: e.copy(out=dumm[:, 0:1],
                               in_=dumm[:, 0:1]), inc=msc)
                PR.wait("tensor", msc, oT_rdy)
                for sr in range(2):
                    bks = [ringF.acquire(), ringF.acquire()]
                    for kt in range(KT):
                        for half in range(2):
                            dgr = sr * 2 + half
                            PR.op("tensor", lambda e, o=bks[half][:, 0:512],
                                  l=oTs[:, kt * P:(kt + 1) * P],
                                  r=wo_sb[:, kt * D + dgr * 512:kt * D + (dgr + 1) * 512],
                                  st=(kt == 0), sp=(kt == KT - 1):
                                  e.matmul(o, l, r, start=st, stop=sp),
                                  inc=mtc if (kt == KT - 1 and half == 1) else None)
                    mm_t = mtc.n
                    PR.wait("vector", mtc, mm_t)
                    if sr == 0 and tt >= 1:
                        PR.wait("vector", dsy, st5_t.get(tt - 1, 0))
                    for half in range(2):
                        dgr = sr * 2 + half
                        sl = slice(dgr * 512, (dgr + 1) * 512)
                        rs = ringF.plan_release()
                        PR.op_preinc("vector", lambda e, o=xnw[:, sl], X=xt5[buf][:, sl],
                              s=bks[half][:, 0:512]: e.tensor_add(out=o, in0=X, in1=s), rs)
                wo_t[tt] = mtc.n
                vend = PR.op("vector", lambda e: e.engine_nop(), inc=mvc)
                PR.wait("sync", mvc, vend)
                PR.op("sync", lambda e, d=x_scr[g], s=xnw: e.dma_start(out=d, in_=s),
                      inc=dsy, k=16)
                st5_t[tt] = dsy.n
            PR.barrier()
            A.release()

        sel = os.environ.get("MK_PHASES", "WABCD")
        nsub = sum(1 for c2 in "ABCD" if c2 in sel)
        for b in range(NB):
            if "A" in sel:
                phase_A(b, with_W=(b == 0 and "W" in sel))
            if "B" in sel:
                phase_ffn(b, x_scr, x_scr if "D" in sel else (y_out if nsub == 2 else x_scr),
                          tag="B")
            if "C" in sel:
                phase_C(b)
            if "D" in sel:
                phase_ffn(b, x_scr, y_out, tag="D")
        if "D" not in sel and ("C" in sel or "B" not in sel or nsub == 1):
            PR.op("sync", lambda e: e.dma_start(out=y_out, in_=x_scr), inc=dsy, k=16)

        PR.wait("sync", dsy, dsy.n)
        PR.wait("gpsimd", dgp, dgp.n)

        block = ctx.enter_context(nc.Block())
        PR.finalize(block)
    return nc



def _prep_inputs(kw):
    f32 = np.float32
    asc = np.ascontiguousarray
    x = np.asarray(kw["L0_attn_a_x"], f32)
    common = dict(
        lhs_A=_lhsT_tiles(asc(np.asarray(kw["L0_attn_a_A_w"], f32).T)),
        lhs_B=_lhsT_tiles(asc(np.asarray(kw["L0_attn_a_B_w"], f32).T)),
        lhs_Ca=_lhsT_tiles(asc(np.asarray(kw["L0_attn_a_Ca_w"], f32).T)),
        lhs_Cb=_lhsT_tiles(asc(np.asarray(kw["L0_attn_a_Cb_w"], f32).T)),
        lhs_Dm=_lhsT_tiles(asc(np.asarray(kw["L0_attn_a_D_w"], f32).T)),
        rhs_outw=_rhs_tiles(asc(np.asarray(kw["L0_attn_a_out_w"], f32).T)),
        rhs_q=_rhs_tiles(asc(np.asarray(kw["L0_attn_b_Wq"], f32).T)),
        rhs_k=_rhs_tiles(asc(np.asarray(kw["L0_attn_b_Wk"], f32).T)),
        rhs_v=_rhs_tiles(asc(np.asarray(kw["L0_attn_b_Wv"], f32).T)),
        rhs_og=_rhs_tiles(asc(np.asarray(kw["L0_attn_b_og_w"], f32).T)),
        rhs_wo=_rhs_tiles(asc(np.asarray(kw["L0_attn_b_Wo"], f32).T)),
        lhs_ig=_lhsT_tiles(asc(np.asarray(kw["L0_attn_b_ig_w"], f32).T)),
        lhs_gw=asc(np.asarray(kw["L0_attn_b_gamma_w"], f32).T.reshape(KT, P, H)),
        gidxT=asc(np.asarray(kw["L0_ffn_gate_idx"], np.int32).T.reshape(KT, P, INTER)),
        uidxT=asc(np.asarray(kw["L0_ffn_up_idx"], np.int32).T.reshape(KT, P, INTER)),
        didxT=asc(np.asarray(kw["L0_ffn_down_idx"], np.int32).T.reshape(IT, P, D)),
        lut_t=_row_bc(np.asarray(kw["L0_ffn_lut"], f32)).astype(f32),
        ab_pp=_pp(np.asarray(kw["L0_attn_a_A_b"], f32)),
        outb_bc=_row_bc(np.asarray(kw["L0_attn_a_out_b"], f32)),
        rms_bc=_row_bc(np.asarray(kw["L0_attn_a_rms_w"], f32)),
        igb_pp=_pp(np.asarray(kw["L0_attn_b_ig_b"], f32)),
        convw_pp=asc(np.asarray(kw["L0_attn_b_conv_w"], f32).reshape(KT, P, 4)
                     .transpose(1, 0, 2).reshape(P, KT * 4)),
        convb_pp=_pp(np.asarray(kw["L0_attn_b_conv_b"], f32)),
        gammab_pp=asc(np.asarray(kw["L0_attn_b_gamma_b"], f32).reshape(16, 1)),
        ogb_bc=_row_bc(np.asarray(kw["L0_attn_b_og_b"], f32)),
        vng_bc=_row_bc(np.tile(np.asarray(kw["L0_attn_b_vn_g"], f32), H)),
        vnb_bc=_row_bc(np.tile(np.asarray(kw["L0_attn_b_vn_b"], f32), H)),
        mng_bc=_row_bc(np.tile(np.asarray(kw["L0_attn_b_mn_g"], f32), H)),
        mnb_bc=_row_bc(np.tile(np.asarray(kw["L0_attn_b_mn_b"], f32), H)),
        gng_bc=_row_bc(np.asarray(kw["L0_attn_b_gn_g"], f32)),
        gnb_bc=_row_bc(np.asarray(kw["L0_attn_b_gn_b"], f32)),
    )
    common = {k2: asc(v) for k2, v in common.items()}
    in_maps = []
    for c in range(8):
        m = dict(common)
        m["x_in"] = asc(x[2 * c:2 * c + 2].reshape(2 * KT, P, D))
        in_maps.append(m)
    return in_maps


def kernel(**kw):
    in_maps = _prep_inputs(kw)
    nc = build_nc()
    res = run_bass_kernel_spmd(nc, in_maps, core_ids=list(range(8)))
    out = np.empty((16, T, D), np.float32)
    for c in range(8):
        out[2 * c:2 * c + 2] = res.results[c]["y_out"].reshape(2, T, D)
    if os.environ.get("MK_DBG", ""):
        import ml_dtypes
        np.save("/tmp/dbg1.npy", np.asarray(res.results[0]["dbg1"], np.float32))
        np.save("/tmp/dbg2.npy", np.asarray(res.results[0]["dbg2"], np.float32))
        np.save("/tmp/dbg3.npy", np.asarray(res.results[0]["dbg3"], np.float32))
    return out
